# version 01
# baseline (speedup 1.0000x reference)
import numpy as np
import ml_dtypes
import concourse.bass as bass
import concourse.mybir as mybir
from concourse.bass_utils import run_bass_kernel_spmd

F32 = mybir.dt.float32
BF16 = mybir.dt.bfloat16
F16 = mybir.dt.float16
U8 = mybir.dt.uint8
U16 = mybir.dt.uint16
ALU = mybir.AluOpType
AF = mybir.ActivationFunctionType

D = 2048
KC = 16
T = 4096
B = 4
S = 8192
DFF = 8192
I = 5632
NCORES = 8

EPS_RMS = 1.1920929e-07
EPS_LN = 1e-5
EPS_MIX = 1e-6
EPS_FFN = 1e-6

ARENA_BYTES = 198 * 1024
SBUF_LIMIT = ARENA_BYTES
PERSIST_BYTES = 8192

ENGINES = ("sync", "tensor", "vector", "scalar", "gpsimd")


class Sem:
    def __init__(self, name):
        self.name = name
        self.n = 0
        self.h = None

    def bump(self, k):
        self.n += k
        return self.n


class Ctx:

    def __init__(self, nc):
        self.nc = nc
        self.plan = {e: [] for e in ENGINES}
        self.sems = {}
        for s in ("ds", "dg", "da", "pe", "ve", "se", "ge", "cc"):
            self.sems[s] = Sem(s)
        self.off = PERSIST_BYTES
        self.sb_count = 0
        base0 = nc.bytes_allocated(None, None) if False else None
        self.arena = nc.alloc_sbuf_tensor("arena", [128, ARENA_BYTES], U8)
        self.arena_base = nc.lookup_mloc(self.arena).addr
        self.dram = {}
        self.psum = [nc.alloc_psum_tensor(f"pb{i}", [128, 512], F32).ap()
                     for i in range(8)]
        self.bank_full = [("pe", 0)] * 8
        self.bank_free = [("ve", 0)] * 8
        self.sq_free = 0

    def do(self, eng, fn, waits=(), inc=(), drain=False):
        tgts = []
        incs = []
        for s, k in inc:
            sem = self.sems[s]
            tgts.append(sem.bump(k))
            incs.append((sem, k, sem.n))
        self.plan[eng].append((list(waits), fn, incs, drain))
        return tgts

    def wait(self, eng, waits):
        self.do(eng, None, waits=waits)

    def sbuf(self, shape, dtype, at=None, align=64):
        nbytes = int(np.prod(shape[1:])) * {F32: 4, BF16: 2, F16: 2, U8: 1, U16: 2}[dtype]
        if at is None:
            at = (self.off + align - 1) // align * align
            self.off = at + nbytes
        assert at + nbytes <= SBUF_LIMIT, f"sbuf overflow at={at} n={nbytes}"
        self.sb_count += 1
        return self.nc.alloc_sbuf_tensor_at(
            f"sb{self.sb_count}", list(shape), dtype, offset=self.arena_base + at).ap()

    def reset_arena(self):
        self.off = PERSIST_BYTES

    def mark(self):
        return self.off

    def release(self, off):
        self.off = off

    def dram_t(self, name, shape, dtype, kind):
        if name not in self.dram:
            self.dram[name] = self.nc.dram_tensor(name, list(shape), dtype, kind=kind).ap()
        return self.dram[name]

    def emit(self):
        nc = self.nc
        import contextlib
        with contextlib.ExitStack() as st:
            for s in self.sems.values():
                s.h = st.enter_context(nc.semaphore(s.name))
            block = st.enter_context(nc.Block())
            for ename in ENGINES:
                recs = self.plan[ename]
                def make(recs):
                    def run(eng):
                        for waits, fn, incs, drain in recs:
                            for sem, val in waits:
                                eng.wait_ge(self.sems[sem].h if isinstance(sem, str) else sem.h, val)
                            inst = fn(eng) if fn is not None else None
                            if incs:
                                if drain or inst is None:
                                    inst = eng.drain()
                                for sem, k, _ in incs:
                                    inst.then_inc(sem.h, k)
                    return run
                getattr(block, ename)(make(recs))


EXT_INPUTS = {
    "x0T": ([D, T], F32),
    "wqkvT": ([D, 3 * D], F32),
    "wsT": ([D, 2 * DFF], F32),
    "woT": ([DFF, D], F32),
    "gub": ([D, 2 * I], U8),
    "gus": ([D, 2 * I], F16),
    "dnb": ([I, D], U8),
    "dns": ([I, D], F16),
    "msel": ([128, 8], F32),
}
BOUNCE = {
    "wqkvT16": ([D, 3 * D], BF16),
    "xaT": ([D, T], F32),
    "sigT": ([D, T], F32),
    "kvcT": ([D, T], F32),
    "wcT": ([D, T], F32),
    "carry": ([128, 32], F32),
    "carryx": ([128, 32], F32),
    "hswT": ([DFF, T], BF16),
    "oswT": ([D, T], F32),
    "x1T": ([D, T], F32),
    "x2aT": ([D, T], BF16),
    "hffT": ([I, T], BF16),
    "yffT": ([D, T], F32),
}
OUTPUT = {"x2T": ([D, T], F32)}

PHASE_PROD = {
    "stage": ["wqkvT16"],
    "A": ["xaT", "sigT", "kvcT", "wcT", "carry"],
    "cc": ["carryx"],
    "B1": ["hswT"],
    "B2": ["oswT"],
    "C": ["x1T", "x2aT"],
    "D1": ["hffT"],
    "D2": ["yffT"],
    "E": ["x2T"],
}
PHASE_CONS = {
    "stage": ["wqkvT"],
    "A": ["x0T", "wqkvT16"],
    "cc": ["carry", "msel"],
    "B1": ["sigT", "kvcT", "wcT", "carryx", "wsT"],
    "B2": ["hswT", "woT"],
    "C": ["oswT", "xaT", "x0T"],
    "D1": ["x2aT", "gub", "gus"],
    "D2": ["hffT", "dnb", "dns"],
    "E": ["yffT", "x1T"],
}

ALL_PHASES = ["stage", "A", "cc", "B1", "B2", "C", "D1", "D2", "E"]
A_NSUB = 16
B2MG = (6, 6, 4)
A_STAGE = 3
GROUPS = [[p] for p in ALL_PHASES]
DEBUG_OUTS = []


def _barrier(ctx):
    c = ctx
    vals = {}
    vals["pe"] = c.do("tensor", lambda e: e.drain(), inc=[("pe", 1)])[0]
    vals["ve"] = c.do("vector", lambda e: e.nop(), inc=[("ve", 1)], drain=True)[0]
    vals["se"] = c.do("scalar", lambda e: e.nop(), inc=[("se", 1)], drain=True)[0]
    vals["ge"] = c.do("gpsimd", lambda e: e.nop(), inc=[("ge", 1)], drain=True)[0]
    ws = [("pe", vals["pe"]), ("ve", vals["ve"]), ("se", vals["se"]), ("ge", vals["ge"])]
    for e in ENGINES:
        c.wait(e, ws)


def plan_consts(ctx):
    c = ctx
    c.ones_col = c.sbuf([128, 512], F32, at=0)
    c.ones_bf = c.sbuf([128, 1], BF16, at=2048)
    c.ones_row = c.sbuf([1, 128], F32, at=2112)
    c.carry_sb = c.sbuf([128, 32], F32, at=2624)
    c.carryx_sb = c.sbuf([128, 32], F32, at=2752)
    c.msel_sb = c.sbuf([128, 8], F32, at=2880)
    c.gath_sb = c.sbuf([128, 256], F32, at=2912 + 96)
    c.par = c.sbuf([128, 1], F32, at=4096)
    c.par1m = c.sbuf([128, 1], F32, at=4160)
    c.epsc = {}
    off = 4224
    for v in (EPS_LN, EPS_RMS, EPS_MIX):
        c.epsc[v] = c.sbuf([128, 1], F32, at=off)
        off += 64
        c.do("vector", (lambda vv: lambda e: e.memset(c.epsc[vv], vv))(v))
    c.do("vector", lambda e: e.memset(c.ones_col, 1.0))
    c.do("vector", lambda e: e.memset(c.ones_bf, 1.0))
    c.do("vector", lambda e: e.memset(c.ones_row, 1.0))
    c.do("vector", lambda e: e.memset(c.carry_sb, 0.0), inc=[("ve", 1)], drain=True)
    v = c.sems["ve"].n
    if "msel" in c.dram:
        t = c.do("sync", lambda e: e.dma_start(out=c.msel_sb, in_=c.dram["msel"]),
                 inc=[("ds", 16)])[0]
        c.wait("vector", [("ds", t)])
    c.wait("tensor", [("ve", v)])
    c.wait("scalar", [("ve", v)])
    c.wait("gpsimd", [("ve", v)])


def dequant_chain(c, b8v, scv, outv, A, B, Cb, waits0=(), inc_last=False):
    c.do("vector", lambda e: e.tensor_copy(A, b8v), waits=list(waits0))
    c.do("vector", lambda e: e.drain())
    c.do("vector", lambda e: e.tensor_scalar(B, A, 15, None, ALU.bitwise_and))
    c.do("vector", lambda e: e.tensor_scalar(Cb, A, 4, None, ALU.logical_shift_right))
    c.do("vector", lambda e: e.drain())
    c.do("vector", lambda e: e.tensor_scalar(A, B, c.par1m, None, ALU.mult))
    c.do("vector", lambda e: e.drain())
    c.do("vector", lambda e: e.scalar_tensor_tensor(B, Cb, c.par, A, ALU.mult, ALU.add))
    c.do("vector", lambda e: e.drain())
    c.do("vector", lambda e: e.tensor_scalar(A, B, 7, 9, ALU.bitwise_and, ALU.logical_shift_left))
    c.do("vector", lambda e: e.tensor_scalar(Cb, B, 8, 12, ALU.bitwise_and, ALU.logical_shift_left))
    c.do("vector", lambda e: e.drain())
    c.do("vector", lambda e: e.tensor_tensor(B, A, Cb, ALU.bitwise_or))
    c.do("vector", lambda e: e.drain())
    c.do("vector", lambda e: e.scalar_tensor_tensor(outv, B.bitcast(F16), 16384.0, scv,
                                                    ALU.mult, ALU.mult),
         inc=([("ve", 1)] if inc_last else []), drain=inc_last)


def plan_stage(ctx):
    c = ctx
    src = c.dram["wqkvT"]
    dst = c.dram["wqkvT16"]
    c.reset_arena()
    buf = [c.sbuf([128, KC * 512], BF16) for _ in range(2)]
    n_sl = (3 * D) // 512
    out_t = [0, 0]
    for i in range(n_sl):
        b = i % 2
        w = [("da", out_t[b])] if out_t[b] else []
        tin = c.do("gpsimd", (lambda sl, bb: lambda e: e.dma_start(
            out=buf[bb].rearrange("p (k t) -> p k t", k=KC),
            in_=src.rearrange("(k p) f -> p k f", p=128)[:, :, sl * 512:(sl + 1) * 512]))(i, b),
            waits=w, inc=[("dg", 16)])[0]
        out_t[b] = c.do("sync", (lambda sl, bb: lambda e: e.dma_start(
            out=dst.rearrange("(k p) f -> p k f", p=128)[:, :, sl * 512:(sl + 1) * 512],
            in_=buf[bb].rearrange("p (k t) -> p k t", k=KC)))(i, b),
            waits=[("dg", tin)], inc=[("da", 16)])[0]
    c.wait("sync", [("da", c.sems["da"].n)])


def plan_A(ctx):
    c = ctx
    NT = 256
    NSUB = A_NSUB
    x0T = c.dram["x0T"]
    wq16 = c.dram["wqkvT16"]
    c.reset_arena()
    x0sb = c.sbuf([128, KC * NT], F32)
    xabf = c.sbuf([128, KC * NT], BF16)
    sqbf = c.sbuf([128, KC * NT], BF16)
    xa32 = c.sbuf([128, KC * NT], F32)
    tsc = c.sbuf([128, KC * NT], F32)
    wbuf = [c.sbuf([128, 64 * 128], BF16) for _ in range(2)]
    qs = c.sbuf([128, KC * NT], F32)
    ks = c.sbuf([128, KC * NT], F32)
    vs = c.sbuf([128, KC * NT], F32)
    sig32 = c.sbuf([128, KC * NT], F32)
    rows = c.sbuf([1, 1536], F32)
    rowq = c.sbuf([1, 512], F32)
    bA = c.sbuf([128, NT], F32)
    bB = c.sbuf([128, NT], F32)
    bP = c.sbuf([128, NT], F32)
    bR = c.sbuf([128, NT], F32)

    ROW_A, ROW_B, ROW_T = 0, 512, 1024

    def row(buf, base, n=NT):
        return buf[0:1, base:base + n]

    xaT, sigT, kvcT, wcT = (c.dram[k] for k in ("xaT", "sigT", "kvcT", "wcT"))

    x3 = lambda t: t.rearrange("p (k n) -> p k n", k=KC)
    b3 = lambda t: t.rearrange("p (o n) -> p o n", o=1).broadcast_to([128, KC, NT])

    c.sig_dma = 0
    x0_t = 0
    scan_done_prev = 0
    xa_dma = 0
    kvwc_dma = 0
    rhs_free = 0
    exp_prev = 0

    def reduce16(src, colbase, inc_last):
        pb7 = c.psum[7]
        for kc in range(KC):
            inc = ([("pe", 1)] if inc_last else []) if kc == KC - 1 else []
            c.do("tensor", (lambda kk: lambda e: e.matmul(
                pb7[0:1, colbase:colbase + NT], c.ones_bf,
                src[:, kk * NT:(kk + 1) * NT], start=(kk == 0), stop=(kk == KC - 1)))(kc),
                inc=inc)

    def rsqrt_row(dst_base_buf, dst_base, src_buf, src_base, eps):
        c.do("scalar", lambda e: e.activation(row(dst_base_buf, dst_base), row(src_buf, src_base), AF.Ln, bias=c.epsc[eps][0:1, 0:1]))
        c.do("scalar", lambda e: e.drain())
        c.do("scalar", lambda e: e.activation(row(dst_base_buf, dst_base), row(dst_base_buf, dst_base), AF.Exp, scale=-0.5),
             inc=[("se", 1)], drain=True)
        return c.sems["se"].n

    def bcast_to(dst_sbuf, src_buf, src_base, wait_se, extra=()):
        t_bc = c.do("tensor", lambda e: e.matmul(c.psum[5][:, 0:NT], c.ones_row,
                                                 row(src_buf, src_base), start=True, stop=True),
                    waits=([("se", wait_se)] if wait_se else []) + list(extra)
                          + ([c.bank_free[5]] if c.bank_free[5][1] else []),
                    inc=[("pe", 1)])[0]
        t_cp = c.do("vector", lambda e: e.tensor_copy(dst_sbuf, c.psum[5][:, 0:NT]),
                    waits=[("pe", t_bc)], inc=[("ve", 1)], drain=True)[0]
        c.bank_free[5] = ("ve", t_cp)
        return t_cp

    def rms_row_bcast(src, eps, bdst, extra=()):
        sqw = list(extra)
        if c.sq_free:
            sqw.append(("pe", c.sq_free))
        t_sq = c.do("scalar", lambda e: e.activation(sqbf, src, AF.Square),
                    waits=sqw, inc=[("se", 1)], drain=True)[0]
        bw = [("se", t_sq)]
        if c.bank_free[7][1]:
            bw.append(c.bank_free[7])
        c.wait("tensor", bw)
        reduce16(sqbf, 0, True)
        t_red = c.sems["pe"].n
        c.sq_free = t_red
        c.wait("vector", [("pe", t_red)])
        c.do("vector", lambda e: e.tensor_scalar(row(rowq, 0), c.psum[7][0:1, 0:NT], 1.0 / D, None, ALU.mult),
             inc=[("ve", 1)], drain=True)
        t_ms = c.sems["ve"].n
        c.bank_free[7] = ("ve", t_ms)
        c.wait("scalar", [("ve", t_ms)])
        t_r = rsqrt_row(rowq, 0, rowq, 0, eps)
        return bcast_to(bdst, rowq, 0, t_r)

    wld_t = [0, 0]
    for s in range(NSUB):
        wb0 = 0
        w = []
        if scan_done_prev:
            w.append(("ve", scan_done_prev))
        x0_t = c.do("sync", (lambda ss: lambda e: e.dma_start(
            out=x3(x0sb),
            in_=x0T.rearrange("(k p) t -> p k t", p=128)[:, :, ss * NT:(ss + 1) * NT]))(s),
            waits=w, inc=[("ds", 16)])[0]
        vw = [("ds", x0_t)]
        if kvwc_dma:
            vw.append(("ds", kvwc_dma))
        if rhs_free:
            vw.append(("pe", rhs_free))
        c.wait("vector", vw)
        t_cast = c.do("vector", lambda e: e.tensor_copy(xabf, x0sb),
                      inc=[("ve", 1)], drain=True)[0]
        sqw = [("ds", x0_t)]
        if c.sq_free:
            sqw.append(("pe", c.sq_free))
        t_sq = c.do("scalar", lambda e: e.activation(sqbf, x0sb, AF.Square),
                    waits=sqw, inc=[("se", 1)], drain=True)[0]
        bw = [("ve", t_cast), ("se", t_sq)]
        if c.bank_free[7][1]:
            bw.append(c.bank_free[7])
        c.wait("tensor", bw)
        reduce16(xabf, 0, False)
        reduce16(sqbf, 256, True)
        t_red = c.sems["pe"].n
        c.sq_free = t_red
        c.wait("vector", [("pe", t_red)])
        c.do("vector", lambda e: e.tensor_scalar(row(rows, ROW_A), c.psum[7][0:1, 0:NT], 1.0 / D, None, ALU.mult))
        c.do("vector", lambda e: e.tensor_scalar(row(rows, ROW_T), c.psum[7][0:1, 256:256 + NT], 1.0 / D, None, ALU.mult))
        c.do("vector", lambda e: e.drain())
        c.do("vector", lambda e: e.tensor_tensor(row(rows, ROW_B), row(rows, ROW_A), row(rows, ROW_A), ALU.mult))
        c.do("vector", lambda e: e.drain())
        c.do("vector", lambda e: e.tensor_sub(row(rows, ROW_T), row(rows, ROW_T), row(rows, ROW_B)),
             inc=[("ve", 1)], drain=True)
        t_var = c.sems["ve"].n
        c.bank_free[7] = ("ve", t_var)
        c.wait("scalar", [("ve", t_var)])
        t_rstd = rsqrt_row(rowq, 0, rows, ROW_T, EPS_LN)
        c.wait("vector", [("se", t_rstd)])
        c.do("vector", lambda e: e.tensor_copy(row(rows, ROW_B), row(rowq, 0)))
        c.do("vector", lambda e: e.drain())
        c.do("vector", lambda e: e.tensor_tensor(row(rows, ROW_A), row(rows, ROW_A), row(rowq, 0), ALU.mult))
        c.do("vector", lambda e: e.drain())
        c.do("vector", lambda e: e.tensor_scalar(row(rows, ROW_A), row(rows, ROW_A), -1.0, None, ALU.mult))
        c.do("vector", lambda e: e.tensor_tensor(row(rows, ROW_T), row(rows, ROW_T), row(rowq, 0), ALU.mult))
        c.do("vector", lambda e: e.drain())
        c.do("vector", lambda e: e.tensor_tensor(row(rows, ROW_T), row(rows, ROW_T), row(rowq, 0), ALU.mult),
             inc=[("ve", 1)], drain=True)
        t_r2 = c.sems["ve"].n
        c.wait("scalar", [("ve", t_r2)])
        t_rho = rsqrt_row(rowq, 0, rows, ROW_T, EPS_RMS)
        t_bA = bcast_to(bA, rows, ROW_B, 0, extra=[("ve", t_r2)])
        t_bB = bcast_to(bB, rows, ROW_A, 0)
        t_bP = bcast_to(bP, rowq, 0, t_rho)
        xw = [("ve", t_bB)]
        if xa_dma:
            xw.append(("ds", xa_dma))
        if exp_prev:
            xw.append(("se", exp_prev))
        c.wait("vector", xw)
        c.do("vector", lambda e: e.tensor_mul(x3(tsc), x3(x0sb), b3(bA)))
        c.do("vector", lambda e: e.drain())
        c.do("vector", lambda e: e.tensor_add(x3(xa32), x3(tsc), b3(bB)), inc=[("ve", 1)], drain=True)
        t_xa = c.sems["ve"].n
        c.do("vector", lambda e: e.tensor_copy(xabf, xa32), inc=[("ve", 1)], drain=True)
        t_rhs = c.sems["ve"].n
        xa_dma = c.do("sync", (lambda ss: lambda e: e.dma_start(
            out=xaT.rearrange("(k p) t -> p k t", p=128)[:, :, ss * NT:(ss + 1) * NT],
            in_=x3(xa32)))(s), waits=[("ve", t_xa)], inc=[("ds", 16)])[0]
        if A_STAGE < 2:
            continue
        ev_t = {}
        for mg in range(12):
            wb = mg % 2
            w = [("pe", wld_t[wb])] if wld_t[wb] else []
            tw = c.do("sync", (lambda mgg, bb: lambda e: e.dma_start(
                out=wbuf[bb].rearrange("p (k f) -> p k f", k=KC),
                in_=wq16.rearrange("(k p) f -> p k f", p=128)[:, :, mgg * 512:(mgg + 1) * 512]))(mg, wb),
                waits=w, inc=[("ds", 16)])[0]
            for mi in range(4):
                m = mg * 4 + mi
                bank = mi % 4
                bw = [("ds", tw), ("ve", t_rhs)]
                if c.bank_free[bank][1]:
                    bw.append(c.bank_free[bank])
                c.wait("tensor", bw)
                for kc in range(KC):
                    inc = [("pe", 1)] if kc == KC - 1 else []
                    c.do("tensor", (lambda mm, kk, bk, bb: lambda e: e.matmul(
                        c.psum[bk][:, 0:NT],
                        wbuf[bb][:, kk * 512 + (mm % 4) * 128:kk * 512 + (mm % 4) * 128 + 128],
                        xabf[:, kk * NT:(kk + 1) * NT],
                        start=(kk == 0), stop=(kk == KC - 1)))(m, kc, bank, wb), inc=inc)
                c.bank_full[bank] = ("pe", c.sems["pe"].n)
                if mi == 3:
                    wld_t[wb] = c.sems["pe"].n
                dst = (qs, ks, vs)[m // KC]
                ch = m % KC
                evw = [c.bank_full[bank], ("ve", t_bP)]
                tf = c.do("vector", (lambda dd, cc2, bk: lambda e: e.tensor_mul(
                    dd[:, cc2 * NT:(cc2 + 1) * NT], c.psum[bk][:, 0:NT], bP))(dst, ch, bank),
                    waits=evw, inc=[("ve", 1)], drain=True)[0]
                c.bank_free[bank] = ("ve", tf)
                ev_t[m // KC] = tf
        rhs_free = c.sems["pe"].n
        if A_STAGE < 3:
            continue
        t_bq = rms_row_bcast(qs, EPS_RMS, bR, extra=[("ve", ev_t[0])])
        c.wait("vector", [("ve", t_bq)])
        c.do("vector", lambda e: e.tensor_mul(x3(tsc), x3(qs), b3(bR)),
             inc=[("ve", 1)], drain=True)
        t_qh = c.sems["ve"].n
        sgw = [("ve", t_qh)]
        if c.sig_dma:
            sgw.append(("ds", c.sig_dma))
        t_sig = c.do("scalar", lambda e: e.activation(sig32, tsc, AF.Sigmoid),
                     waits=sgw, inc=[("se", 1)], drain=True)[0]
        c.sig_dma = c.do("sync", (lambda ss: lambda e: e.dma_start(
            out=sigT.rearrange("(k p) t -> p k t", p=128)[:, :, ss * NT:(ss + 1) * NT],
            in_=x3(sig32)))(s), waits=[("se", t_sig)], inc=[("ds", 16)])[0]
        t_bk = rms_row_bcast(ks, EPS_RMS, bR, extra=[("ve", ev_t[1])])
        c.wait("vector", [("ve", t_bk), ("se", t_sig)])
        c.do("vector", lambda e: e.tensor_mul(x3(tsc), x3(ks), b3(bR)),
             inc=[("ve", 1)], drain=True)
        t_kh = c.sems["ve"].n
        t_exp = c.do("scalar", lambda e: e.activation(x0sb, tsc, AF.Exp),
                     waits=[("ve", t_kh)], inc=[("se", 1)], drain=True)[0]
        exp_prev = t_exp
        c.wait("vector", [("se", t_exp), ("se", t_sig)])
        c.do("vector", lambda e: e.tensor_mul(qs, x0sb, vs), inc=[("ve", 1)], drain=True)
        t_wv = c.sems["ve"].n
        sw = [("ve", t_wv)]
        if kvwc_dma:
            sw.append(("ds", kvwc_dma))
        c.wait("vector", sw)
        for kc in range(KC):
            c.do("vector", (lambda kk: lambda e: e.tensor_tensor_scan(
                vs[:, kk * NT:(kk + 1) * NT], c.ones_col[:, 0:NT],
                qs[:, kk * NT:(kk + 1) * NT], c.carry_sb[:, kk:kk + 1],
                ALU.mult, ALU.add))(kc))
        for kc in range(KC):
            c.do("vector", (lambda kk: lambda e: e.tensor_tensor_scan(
                ks[:, kk * NT:(kk + 1) * NT], c.ones_col[:, 0:NT],
                x0sb[:, kk * NT:(kk + 1) * NT], c.carry_sb[:, 16 + kk:17 + kk],
                ALU.mult, ALU.add))(kc))
        c.do("vector", lambda e: e.drain())
        c.do("vector", lambda e: e.tensor_copy(
            c.carry_sb[:, 0:16], x3(vs)[:, :, NT - 1]))
        c.do("vector", lambda e: e.tensor_copy(
            c.carry_sb[:, 16:32], x3(ks)[:, :, NT - 1]),
            inc=[("ve", 1)], drain=True)
        scan_done_prev = c.sems["ve"].n
        c.do("sync", (lambda ss: lambda e: e.dma_start(
            out=kvcT.rearrange("(k p) t -> p k t", p=128)[:, :, ss * NT:(ss + 1) * NT],
            in_=x3(vs)))(s), waits=[("ve", scan_done_prev)], inc=[("ds", 16)])
        kvwc_dma = c.do("sync", (lambda ss: lambda e: e.dma_start(
            out=wcT.rearrange("(k p) t -> p k t", p=128)[:, :, ss * NT:(ss + 1) * NT],
            in_=x3(ks)))(s), inc=[("ds", 16)])[0]
    c.do("gpsimd", lambda e: e.dma_start(out=c.dram["carry"], in_=c.carry_sb),
         waits=[("ve", scan_done_prev)], inc=[("dg", 16)])
    c.wait("gpsimd", [("dg", c.sems["dg"].n)])
    c.wait("sync", [("ds", c.sems["ds"].n)])


def plan_cc(ctx):
    c = ctx
    cc_in = c.dram_t("cc_in", [128, 32], F32, "Internal")
    cc_out = c.dram_t("cc_out", [NCORES * 128, 32], F32, "Internal")
    tb = c.do("gpsimd", lambda e: e.dma_start(out=cc_in, in_=c.dram["carry"]),
              inc=[("dg", 16)])[0]
    t = c.do("gpsimd", lambda e: e.collective_compute(
        "AllGather", ALU.bypass, replica_groups=[list(range(NCORES))],
        ins=[cc_in.opt()], outs=[cc_out.opt()]),
        waits=[("dg", tb)], inc=[("cc", 1)])[0]
    tg = c.do("gpsimd", lambda e: e.dma_start(
        out=c.gath_sb.rearrange("p (r f) -> p r f", r=NCORES),
        in_=cc_out.rearrange("(r p) f -> p r f", p=128)),
        waits=[("cc", t)], inc=[("dg", 16)])[0]
    c.wait("vector", [("dg", tg)])
    c.do("vector", lambda e: e.tensor_scalar(
        c.carryx_sb, c.gath_sb[:, 0:32], c.msel_sb[:, 0:1], None, ALU.mult))
    for rk in range(1, NCORES):
        c.do("vector", (lambda r: lambda e: e.scalar_tensor_tensor(
            c.carryx_sb, c.gath_sb[:, r * 32:(r + 1) * 32], c.msel_sb[:, r:r + 1],
            c.carryx_sb, ALU.mult, ALU.add))(rk))
        c.do("vector", lambda e: e.drain())
    c.do("vector", lambda e: e.drain(), inc=[("ve", 1)])
    tv = c.sems["ve"].n
    c.do("gpsimd", lambda e: e.dma_start(out=c.dram["carryx"], in_=c.carryx_sb),
         waits=[("ve", tv)], inc=[("dg", 16)])
    c.wait("gpsimd", [("dg", c.sems["dg"].n)])


def plan_load_carryx(ctx):
    c = ctx
    t = c.do("sync", lambda e: e.dma_start(out=c.carryx_sb, in_=c.dram["carryx"]),
             inc=[("ds", 16)])[0]
    c.wait("vector", [("ds", t)])


def plan_B1(ctx):
    c = ctx
    NT = 128
    x3 = lambda t, n=NT: t.rearrange("p (k n) -> p k n", k=KC)
    c.reset_arena()
    ybf = c.sbuf([128, KC * T], BF16)
    _mk = c.mark()
    kv = c.sbuf([128, KC * NT], F32)
    wc = c.sbuf([128, KC * NT], F32)
    sg = c.sbuf([128, KC * NT], F32)
    t1 = c.sbuf([128, KC * NT], F32)
    kvcT, wcT, sigT = c.dram["kvcT"], c.dram["wcT"], c.dram["sigT"]
    ld_t = 0
    for j in range(T // NT):
        tds = []
        for (dt_, sb) in ((kvcT, kv), (wcT, wc), (sigT, sg)):
            tds.append(c.do("sync", (lambda dd, ss, jj: lambda e: e.dma_start(
                out=x3(ss), in_=dd.rearrange("(k p) t -> p k t", p=128)[:, :, jj * NT:(jj + 1) * NT]))(dt_, sb, j),
                waits=([("ve", ld_t)] if ld_t else []), inc=[("ds", 16)])[0])
        c.wait("vector", [("ds", tds[-1])])
        for kc in range(KC):
            c.do("vector", (lambda kk: lambda e: e.tensor_scalar(
                kv[:, kk * NT:(kk + 1) * NT], kv[:, kk * NT:(kk + 1) * NT],
                c.carryx_sb[:, kk:kk + 1], None, ALU.add))(kc))
        for kc in range(KC):
            c.do("vector", (lambda kk: lambda e: e.tensor_scalar(
                wc[:, kk * NT:(kk + 1) * NT], wc[:, kk * NT:(kk + 1) * NT],
                c.carryx_sb[:, 16 + kk:17 + kk], None, ALU.add))(kc))
        c.do("vector", lambda e: e.drain(), inc=[("ve", 1)])
        t_ca = c.sems["ve"].n
        c.wait("scalar", [("ve", t_ca)])
        c.do("scalar", lambda e: e.activation(t1, wc, AF.Ln, bias=c.epsc[EPS_MIX]))
        c.do("scalar", lambda e: e.drain())
        c.do("scalar", lambda e: e.activation(t1, t1, AF.Exp, scale=-1.0),
             inc=[("se", 1)], drain=True)
        t_r = c.sems["se"].n
        c.wait("vector", [("se", t_r)])
        c.do("vector", lambda e: e.tensor_mul(kv, kv, t1))
        c.do("vector", lambda e: e.drain())
        c.do("vector", (lambda jj: lambda e: e.tensor_mul(
            ybf[:, jj * NT:(jj + 1) * NT].rearrange("p n -> p n") if False else
            ybf.rearrange("p (k t) -> p k t", k=KC)[:, :, jj * NT:(jj + 1) * NT],
            x3(kv), x3(sg)))(j), inc=[("ve", 1)], drain=True)
        ld_t = c.sems["ve"].n
    y_ready = ld_t
    c.release(_mk)
    wsT = c.dram["wsT"]
    hswT = c.dram["hswT"]
    wtile = [c.sbuf([128, 2 * KC * 128], BF16) for _ in range(2)]
    hrow = [c.sbuf([128, T], BF16) for _ in range(2)]
    sl = c.sbuf([128, 512], F32)
    wld = [0, 0]
    hdma = [0, 0]
    NTM = 512
    y3 = ybf.rearrange("p (k t) -> p k t", k=KC)
    for mu in range(DFF // 128):
        wb = mu % 2
        w = [("pe", wld[wb])] if wld[wb] else [("ve", y_ready)]
        ws3 = wsT.rearrange("(k p) f -> p k f", p=128)
        c.do("gpsimd", (lambda m, bb: lambda e: e.dma_start(
            out=wtile[bb].rearrange("p (u kf) -> p u kf", u=2)[:, 0, :]
                .rearrange("p (k f) -> p k f", k=KC),
            in_=ws3[:, :, m * 128:(m + 1) * 128]))(mu, wb),
            waits=w, inc=[("dg", 16)])
        tw = c.do("gpsimd", (lambda m, bb: lambda e: e.dma_start(
            out=wtile[bb].rearrange("p (u kf) -> p u kf", u=2)[:, 1, :]
                .rearrange("p (k f) -> p k f", k=KC),
            in_=ws3[:, :, (DFF + m * 128):(DFF + (m + 1) * 128)]))(mu, wb),
            inc=[("dg", 16)])[0]
        hb = mu % 2
        for n in range(T // NTM):
            for ug in range(2):
                bank = (n * 2 + ug) % 4
                bw = [("dg", tw), ("ve", y_ready)]
                if c.bank_free[bank][1]:
                    bw.append(c.bank_free[bank])
                c.wait("tensor", bw)
                for kc in range(KC):
                    inc = [("pe", 1)] if kc == KC - 1 else []
                    c.do("tensor", (lambda bk, uu, kk, nn, bb: lambda e: e.matmul(
                        c.psum[bk][:, 0:NTM],
                        wtile[bb][:, uu * KC * 128 + kk * 128:uu * KC * 128 + (kk + 1) * 128],
                        y3[:, kk, nn * NTM:(nn + 1) * NTM],
                        start=(kk == 0), stop=(kk == KC - 1)))(bank, ug, kc, n, wb), inc=inc)
                c.bank_full[bank] = ("pe", c.sems["pe"].n)
            bku, bkg = (n * 2) % 4, (n * 2 + 1) % 4
            t_sl = c.do("scalar", (lambda bk: lambda e: e.activation(
                sl[:, 0:NTM], c.psum[bk][:, 0:NTM], AF.Silu))(bkg),
                waits=[c.bank_full[bkg]], inc=[("se", 1)], drain=True)[0]
            c.bank_free[bkg] = ("se", t_sl)
            hw = [c.bank_full[bku], ("se", t_sl)]
            if hdma[hb]:
                hw.append(("da", hdma[hb]))
                hdma[hb] = 0
            tf = c.do("vector", (lambda bk, nn, bb: lambda e: e.tensor_mul(
                hrow[bb][:, nn * NTM:(nn + 1) * NTM], c.psum[bk][:, 0:NTM], sl[:, 0:NTM]))(bku, n, hb),
                waits=hw, inc=[("ve", 1)], drain=True)[0]
            c.bank_free[bku] = ("ve", tf)
            if mu >= 62:
                wld[wb] = 0
        wld[wb] = c.sems["pe"].n
        hdma[hb] = c.do("scalar", (lambda m, bb: lambda e: e.dma_start(
            out=hswT[m * 128:(m + 1) * 128, :], in_=hrow[bb]))(mu, hb),
            waits=[("ve", c.sems["ve"].n)], inc=[("da", 16)])[0]
    c.wait("scalar", [("da", c.sems["da"].n)])


def plan_mm_bigK(ctx, w_dram, w_cast_from_f32, KD, hT_dram, out_dram, deq=None,
                 mgroups=(8, 8)):
    c = ctx
    KCH = KD
    NT = 256
    MM = max(mgroups)
    wres = c.sbuf([128, KCH * MM * 128], BF16)
    hsl = [c.sbuf([128, KCH * NT], BF16) for _ in range(2)]
    osl = c.sbuf([128, MM * NT], F32)
    if deq is not None:
        dq_by = c.sbuf([128, MM * 128], U16)
        dq_b8 = c.sbuf([128, MM * 128], U8)
        dq_t1 = c.sbuf([128, MM * 128], U16)
        dq_t2 = c.sbuf([128, MM * 128], U16)
        dq_sc = c.sbuf([128, MM * 128], F16)
    out3 = out_dram.rearrange("(m p) t -> p m t", p=128)
    h3 = hT_dram.rearrange("(k p) t -> p k t", p=128)
    w3 = (w_dram.rearrange("(k p) f -> p k f", p=128)
          .rearrange("p k (m f) -> p m k f", f=128)) if deq is None else None
    osl_t = 0
    mbase = 0
    pass_pe = 0
    hld = [0, 0]
    for half, MH in enumerate(mgroups):
        pw = [("pe", pass_pe)] if pass_pe else []
        if deq is None:
            for mi in range(MH):
                tw = c.do("gpsimd", (lambda hh, m2: lambda e: e.dma_start(
                    out=wres.rearrange("p (k m f) -> p m k f", k=KCH, m=MM)[:, m2, :, :],
                    in_=w3[:, hh + m2, :, :]))(mbase, mi),
                    waits=(pw if mi == 0 else []), inc=[("dg", 16)])[0]
            c.wait("tensor", [("dg", tw)])
            w_ready = ("dg", tw)
        else:
            bytes_d, scales_d = deq
            by3 = bytes_d.rearrange("(k p) f -> p k f", p=128)
            sc3 = scales_d.rearrange("(k p) f -> p k f", p=128)
            dqv = 0
            for kc in range(KCH):
                bwl = list(pw) if kc == 0 else ([("ve", dqv)] if dqv else [])
                tb = c.do("gpsimd", (lambda kk, hh, mh: lambda e: e.dma_start(
                    out=dq_b8[:, 0:mh * 128], in_=by3[:, kk, hh * 128:hh * 128 + mh * 128]))(kc, mbase, MH),
                    waits=bwl, inc=[("dg", 16)])[0]
                ts = c.do("sync", (lambda kk, hh, mh: lambda e: e.dma_start(
                    out=dq_sc[:, 0:mh * 128], in_=sc3[:, kk, hh * 128:hh * 128 + mh * 128]))(kc, mbase, MH),
                    waits=bwl, inc=[("ds", 16)])[0]
                c.wait("vector", [("dg", tb), ("ds", ts)] + (pw if kc == 0 else []))
                M128 = MH * 128
                dequant_chain(c, dq_b8[:, 0:M128], dq_sc[:, 0:M128],
                              wres[:, kc * MM * 128:kc * MM * 128 + M128],
                              dq_by[:, 0:M128], dq_t1[:, 0:M128], dq_t2[:, 0:M128],
                              inc_last=True)
                dqv = c.sems["ve"].n
            w_ready = ("ve", c.sems["ve"].n)
            c.wait("tensor", [w_ready])
        for n in range(T // NT):
            hb = n % 2
            w = [("ve", hld[hb])] if hld[hb] else []
            th = c.do("sync", (lambda nn, bb: lambda e: e.dma_start(
                out=hsl[bb].rearrange("p (k t) -> p k t", k=KCH),
                in_=h3[:, :, nn * NT:(nn + 1) * NT]))(n, hb),
                waits=w, inc=[("ds", 16)])[0]
            for m in range(MH):
                bank = m
                bw = [("ds", th)]
                if c.bank_free[bank][1]:
                    bw.append(c.bank_free[bank])
                c.wait("tensor", bw)
                for kc in range(KCH):
                    inc = [("pe", 1)] if kc == KCH - 1 else []
                    c.do("tensor", (lambda mm, kk, bb, bk: lambda e: e.matmul(
                        c.psum[bk][:, 0:NT],
                        wres[:, kk * MM * 128 + mm * 128:kk * MM * 128 + (mm + 1) * 128],
                        hsl[bb][:, kk * NT:(kk + 1) * NT],
                        start=(kk == 0), stop=(kk == KCH - 1)))(m, kc, hb, bank), inc=inc)
                c.bank_full[bank] = ("pe", c.sems["pe"].n)
            evw = [c.bank_full[m] for m in range(MH)]
            if osl_t:
                evw.append(("da", osl_t))
            c.wait("vector", evw)
            for m in range(MH):
                inc = [("ve", 1)] if m == MH - 1 else []
                c.do("vector", (lambda mm: lambda e: e.tensor_copy(
                    osl[:, mm * NT:(mm + 1) * NT], c.psum[mm][:, 0:NT]))(m),
                    inc=inc, drain=(m == MH - 1))
            tf = c.sems["ve"].n
            for m in range(MH):
                c.bank_free[m] = ("ve", tf)
            hld[hb] = tf
            osl_t = c.do("scalar", (lambda nn, hh, mh: lambda e: e.dma_start(
                out=out3[:, hh:hh + mh, nn * NT:(nn + 1) * NT],
                in_=osl.rearrange("p (m t) -> p m t", m=MM)[:, 0:mh, :]))(n, mbase, MH),
                waits=[("ve", tf)], inc=[("da", 16)])[0]
        mbase += MH
        pass_pe = c.sems["pe"].n
    c.wait("scalar", [("da", c.sems["da"].n)])


def plan_B2(ctx):
    ctx.reset_arena()
    plan_mm_bigK(ctx, ctx.dram["woT"], True, DFF // 128, ctx.dram["hswT"], ctx.dram["oswT"],
                 mgroups=B2MG)


def plan_epilogue(ctx, main_in, res_in, out_f32, out_bf, rms_eps=None):
    raise NotImplementedError


def plan_C(ctx):
    c = ctx
    NT = 256
    c.reset_arena()
    x3 = lambda t: t.rearrange("p (k n) -> p k n", k=KC)
    b3 = lambda t: t.rearrange("p (o n) -> p o n", o=1).broadcast_to([128, KC, NT])
    osw = c.sbuf([128, KC * NT], F32)
    xas = c.sbuf([128, KC * NT], F32)
    x0s = c.sbuf([128, KC * NT], F32)
    ao = c.sbuf([128, KC * NT], F32)
    tsc = c.sbuf([128, KC * NT], F32)
    x1f = c.sbuf([128, KC * NT], F32)
    x2b = c.sbuf([128, KC * NT], BF16)
    aobf = c.sbuf([128, KC * NT], BF16)
    sqbf = c.sbuf([128, KC * NT], BF16)
    rows = c.sbuf([1, 1536], F32)
    rowq = c.sbuf([1, 512], F32)
    bA = c.sbuf([128, NT], F32)
    bB = c.sbuf([128, NT], F32)
    oswT, xaT, x0T = c.dram["oswT"], c.dram["xaT"], c.dram["x0T"]
    x1T, x2aT = c.dram["x1T"], c.dram["x2aT"]

    def row(buf, base, n=NT):
        return buf[0:1, base:base + n]

    def ln_rows(src_bf, src_sq, eps):
        pb7 = c.psum[7]
        for kc in range(KC):
            c.do("tensor", (lambda kk: lambda e: e.matmul(
                pb7[0:1, 0:NT], c.ones_bf, src_bf[:, kk * NT:(kk + 1) * NT],
                start=(kk == 0), stop=(kk == KC - 1)))(kc))
        for kc in range(KC):
            inc = [("pe", 1)] if kc == KC - 1 else []
            c.do("tensor", (lambda kk: lambda e: e.matmul(
                pb7[0:1, 256:256 + NT], c.ones_bf, src_sq[:, kk * NT:(kk + 1) * NT],
                start=(kk == 0), stop=(kk == KC - 1)))(kc), inc=inc)
        t_red = c.sems["pe"].n
        c.wait("vector", [("pe", t_red)])
        c.do("vector", lambda e: e.tensor_scalar(row(rows, 0), pb7[0:1, 0:NT], 1.0 / D, None, ALU.mult))
        c.do("vector", lambda e: e.tensor_scalar(row(rows, 1024), pb7[0:1, 256:256 + NT], 1.0 / D, None, ALU.mult))
        c.do("vector", lambda e: e.drain())
        c.do("vector", lambda e: e.tensor_tensor(row(rows, 512), row(rows, 0), row(rows, 0), ALU.mult))
        c.do("vector", lambda e: e.drain())
        c.do("vector", lambda e: e.tensor_sub(row(rows, 1024), row(rows, 1024), row(rows, 512)),
             inc=[("ve", 1)], drain=True)
        t_var = c.sems["ve"].n
        c.wait("scalar", [("ve", t_var)])
        c.do("scalar", lambda e: e.activation(row(rowq, 0), row(rows, 1024), AF.Ln, bias=c.epsc[eps][0:1, 0:1]))
        c.do("scalar", lambda e: e.drain())
        c.do("scalar", lambda e: e.activation(row(rowq, 0), row(rowq, 0), AF.Exp, scale=-0.5),
             inc=[("se", 1)], drain=True)
        t_r = c.sems["se"].n
        c.wait("vector", [("se", t_r)])
        c.do("vector", lambda e: e.tensor_tensor(row(rows, 0), row(rows, 0), row(rowq, 0), ALU.mult))
        c.do("vector", lambda e: e.drain())
        c.do("vector", lambda e: e.tensor_scalar(row(rows, 0), row(rows, 0), -1.0, None, ALU.mult),
             inc=[("ve", 1)], drain=True)
        t_b = c.sems["ve"].n
        pb6 = c.psum[6]
        c.wait("tensor", [("ve", t_b)])
        c.do("tensor", lambda e: e.matmul(pb6[:, 0:NT], c.ones_row, row(rowq, 0), start=True, stop=True))
        c.do("tensor", lambda e: e.matmul(pb6[:, 256:256 + NT], c.ones_row, row(rows, 0), start=True, stop=True),
             inc=[("pe", 1)])
        t_bc = c.sems["pe"].n
        c.wait("vector", [("pe", t_bc)])
        c.do("vector", lambda e: e.tensor_copy(bA, pb6[:, 0:NT]))
        c.do("vector", lambda e: e.tensor_copy(bB, pb6[:, 256:256 + NT]), inc=[("ve", 1)], drain=True)
        return c.sems["ve"].n

    st_t = 0
    for j in range(T // NT):
        tds = []
        for (dd, ss) in ((oswT, osw), (xaT, xas), (x0T, x0s)):
            tds.append(c.do("sync", (lambda d2, s2, jj: lambda e: e.dma_start(
                out=x3(s2), in_=d2.rearrange("(k p) t -> p k t", p=128)[:, :, jj * NT:(jj + 1) * NT]))(dd, ss, j),
                waits=([("ve", st_t)] if st_t else []), inc=[("ds", 16)])[0])
        c.wait("vector", [("ds", tds[-1])])
        c.do("vector", lambda e: e.tensor_add(ao, osw, xas), inc=[("ve", 1)], drain=True)
        t_ao = c.sems["ve"].n
        c.do("vector", lambda e: e.tensor_copy(aobf, ao), inc=[("ve", 1)], drain=True)
        t_aob = c.sems["ve"].n
        t_sq = c.do("scalar", lambda e: e.activation(sqbf, ao, AF.Square),
                    waits=[("ve", t_ao)], inc=[("se", 1)], drain=True)[0]
        c.wait("tensor", [("ve", t_aob), ("se", t_sq)])
        t_ab = ln_rows(aobf, sqbf, EPS_LN)
        c.wait("vector", [("ve", t_ab)])
        c.do("vector", lambda e: e.tensor_mul(x3(tsc), x3(ao), b3(bA)))
        c.do("vector", lambda e: e.drain())
        c.do("vector", lambda e: e.tensor_add(x3(tsc), x3(tsc), b3(bB)))
        c.do("vector", lambda e: e.drain())
        c.do("vector", lambda e: e.tensor_add(x1f, tsc, x0s), inc=[("ve", 1)], drain=True)
        t_x1 = c.sems["ve"].n
        c.do("sync", (lambda jj: lambda e: e.dma_start(
            out=x1T.rearrange("(k p) t -> p k t", p=128)[:, :, jj * NT:(jj + 1) * NT],
            in_=x3(x1f)))(j), waits=[("ve", t_x1)], inc=[("ds", 16)])
        c.do("vector", lambda e: e.tensor_copy(aobf, x1f), inc=[("ve", 1)], drain=True)
        t_x1b = c.sems["ve"].n
        t_sq2 = c.do("scalar", lambda e: e.activation(sqbf, x1f, AF.Square),
                     waits=[("ve", t_x1)], inc=[("se", 1)], drain=True)[0]
        c.wait("tensor", [("ve", t_x1b), ("se", t_sq2)])
        t_ab2 = ln_rows(aobf, sqbf, EPS_LN)
        c.wait("vector", [("ve", t_ab2)])
        c.do("vector", lambda e: e.tensor_mul(x3(tsc), x3(x1f), b3(bA)))
        c.do("vector", lambda e: e.drain())
        c.do("vector", lambda e: e.tensor_add(x3(x2b), x3(tsc), b3(bB)), inc=[("ve", 1)], drain=True)
        st_t = c.sems["ve"].n
        c.do("sync", (lambda jj: lambda e: e.dma_start(
            out=x2aT.rearrange("(k p) t -> p k t", p=128)[:, :, jj * NT:(jj + 1) * NT],
            in_=x3(x2b)))(j), waits=[("ve", st_t)], inc=[("ds", 16)])
    c.wait("sync", [("ds", c.sems["ds"].n)])


def plan_D1(ctx):
    c = ctx
    c.reset_arena()
    x2aT = c.dram["x2aT"]
    gub, gus = c.dram["gub"], c.dram["gus"]
    hffT = c.dram["hffT"]
    xres = c.sbuf([128, KC * T], BF16)
    NP = I // 128
    dq_b8 = [c.sbuf([128, KC * 256], U8) for _ in range(2)]
    dq_sc = c.sbuf([128, KC * 256], F16)
    dq_by = c.sbuf([128, KC * 128], U16)
    dq_t1 = c.sbuf([128, KC * 128], U16)
    dq_t2 = c.sbuf([128, KC * 128], U16)
    wtile = [c.sbuf([128, KC * 256], BF16) for _ in range(2)]
    hrow = c.sbuf([128, T], BF16)
    sl = c.sbuf([128, 512], F32)
    t_x = c.do("sync", lambda e: e.dma_start(
        out=xres.rearrange("p (k t) -> p k t", k=KC),
        in_=x2aT.rearrange("(k p) t -> p k t", p=128)), inc=[("ds", 16)])[0]
    by3 = gub.rearrange("(k p) f -> p k f", p=128)
    sc3 = gus.rearrange("(k p) f -> p k f", p=128)
    x33 = xres.rearrange("p (k t) -> p k t", k=KC)
    NTM = 512
    ld = [0, 0]
    sc_free = 0
    wrd = [0, 0]
    hdma = 0
    by2 = lambda b: dq_b8[b].rearrange("p (k u f) -> p k u f", k=KC, u=2)
    sc2 = dq_sc.rearrange("p (k u f) -> p k u f", k=KC, u=2)
    wt2 = lambda b: wtile[b].rearrange("p (k u f) -> p k u f", k=KC, u=2)
    byv = dq_by.rearrange("p (k f) -> p k f", k=KC)
    t1v = dq_t1.rearrange("p (k f) -> p k f", k=KC)
    t2v = dq_t2.rearrange("p (k f) -> p k f", k=KC)
    for mp in range(NP):
        wb = mp % 2
        for ug in range(2):
            col = (mp if ug == 0 else NP + mp) * 128
            w = ([("ve", ld[wb])] if ld[wb] else []) if ug == 0 else []
            tb = c.do("gpsimd", (lambda cc2, bb, uu: lambda e: e.dma_start(
                out=by2(bb)[:, :, uu, :], in_=by3[:, :, cc2:cc2 + 128]))(col, wb, ug),
                waits=w, inc=[("dg", 16)])[0]
            ws = ([("ve", sc_free)] if sc_free else []) if ug == 0 else []
            ts = c.do("sync", (lambda cc2, uu: lambda e: e.dma_start(
                out=sc2[:, :, uu, :], in_=sc3[:, :, cc2:cc2 + 128]))(col, ug),
                waits=ws, inc=[("ds", 16)])[0]
        c.wait("vector", [("dg", tb), ("ds", ts)])
        if wrd[wb]:
            c.wait("vector", [("pe", wrd[wb])])
        for ug in range(2):
            dequant_chain(c, by2(wb)[:, :, ug, :], sc2[:, :, ug, :], wt2(wb)[:, :, ug, :],
                          byv, t1v, t2v, inc_last=(ug == 1))
        ld[wb] = c.sems["ve"].n
        sc_free = c.sems["ve"].n
        t_w = ld[wb]
        for n in range(T // NTM):
            for ug in range(2):
                bank = (n * 2 + ug) % 4
                bw = [("ve", t_w), ("ds", t_x)]
                if c.bank_free[bank][1]:
                    bw.append(c.bank_free[bank])
                c.wait("tensor", bw)
                for kc in range(KC):
                    inc = [("pe", 1)] if kc == KC - 1 else []
                    c.do("tensor", (lambda bk, uu, kk, nn, bb: lambda e: e.matmul(
                        c.psum[bk][:, 0:NTM],
                        wtile[bb][:, kk * 256 + uu * 128:kk * 256 + uu * 128 + 128],
                        x33[:, kk, nn * NTM:(nn + 1) * NTM],
                        start=(kk == 0), stop=(kk == KC - 1)))(bank, ug, kc, n, wb), inc=inc)
                c.bank_full[bank] = ("pe", c.sems["pe"].n)
            wrd[wb] = c.sems["pe"].n
            bkg, bku = (n * 2) % 4, (n * 2 + 1) % 4
            t_sl = c.do("scalar", (lambda bk: lambda e: e.activation(
                sl[:, 0:NTM], c.psum[bk][:, 0:NTM], AF.Silu))(bkg),
                waits=[c.bank_full[bkg]], inc=[("se", 1)], drain=True)[0]
            c.bank_free[bkg] = ("se", t_sl)
            hw = [c.bank_full[bku], ("se", t_sl)]
            if hdma and n == 0:
                hw.append(("da", hdma))
            tf = c.do("vector", (lambda bk, nn: lambda e: e.tensor_mul(
                hrow[:, nn * NTM:(nn + 1) * NTM], c.psum[bk][:, 0:NTM], sl[:, 0:NTM]))(bku, n),
                waits=hw, inc=[("ve", 1)], drain=True)[0]
            c.bank_free[bku] = ("ve", tf)
        hdma = c.do("scalar", (lambda m: lambda e: e.dma_start(
            out=hffT[m * 128:(m + 1) * 128, :], in_=hrow))(mp),
            waits=[("ve", c.sems["ve"].n)], inc=[("da", 16)])[0]
    c.wait("scalar", [("da", c.sems["da"].n)])


def plan_D2(ctx):
    c = ctx
    c.reset_arena()
    plan_mm_bigK(c, None, False, I // 128, c.dram["hffT"], c.dram["yffT"],
                 deq=(c.dram["dnb"], c.dram["dns"]))


def plan_E(ctx):
    c = ctx
    NT = 256
    c.reset_arena()
    x3 = lambda t: t.rearrange("p (k n) -> p k n", k=KC)
    b3 = lambda t: t.rearrange("p (o n) -> p o n", o=1).broadcast_to([128, KC, NT])
    yf = c.sbuf([128, KC * NT], F32)
    x1s = c.sbuf([128, KC * NT], F32)
    tsc = c.sbuf([128, KC * NT], F32)
    x2f = c.sbuf([128, KC * NT], F32)
    ybf = c.sbuf([128, KC * NT], BF16)
    sqbf = c.sbuf([128, KC * NT], BF16)
    rows = c.sbuf([1, 2048], F32)
    rowq = c.sbuf([1, 512], F32)
    bA = c.sbuf([128, NT], F32)
    bB = c.sbuf([128, NT], F32)
    yffT, x1T, x2T = c.dram["yffT"], c.dram["x1T"], c.dram["x2T"]

    def row(base, n=NT):
        return rows[0:1, base:base + n]

    st_t = 0
    for j in range(T // NT):
        tds = []
        for (dd, ss) in ((yffT, yf), (x1T, x1s)):
            tds.append(c.do("sync", (lambda d2, s2, jj: lambda e: e.dma_start(
                out=x3(s2), in_=d2.rearrange("(k p) t -> p k t", p=128)[:, :, jj * NT:(jj + 1) * NT]))(dd, ss, j),
                waits=([("ve", st_t)] if st_t else []), inc=[("ds", 16)])[0])
        c.wait("vector", [("ds", tds[-1])])
        c.do("vector", lambda e: e.tensor_copy(ybf, yf), inc=[("ve", 1)], drain=True)
        t_yb = c.sems["ve"].n
        t_sq = c.do("scalar", lambda e: e.activation(sqbf, yf, AF.Square),
                    waits=[("ds", tds[0])], inc=[("se", 1)], drain=True)[0]
        pb7 = c.psum[7]
        c.wait("tensor", [("ve", t_yb), ("se", t_sq)])
        for kc in range(KC):
            c.do("tensor", (lambda kk: lambda e: e.matmul(
                pb7[0:1, 0:NT], c.ones_bf, ybf[:, kk * NT:(kk + 1) * NT],
                start=(kk == 0), stop=(kk == KC - 1)))(kc))
        for kc in range(KC):
            inc = [("pe", 1)] if kc == KC - 1 else []
            c.do("tensor", (lambda kk: lambda e: e.matmul(
                pb7[0:1, 256:256 + NT], c.ones_bf, sqbf[:, kk * NT:(kk + 1) * NT],
                start=(kk == 0), stop=(kk == KC - 1)))(kc), inc=inc)
        t_red = c.sems["pe"].n
        c.wait("vector", [("pe", t_red)])
        c.do("vector", lambda e: e.tensor_scalar(row(0), pb7[0:1, 0:NT], 1.0 / D, None, ALU.mult))
        c.do("vector", lambda e: e.tensor_scalar(row(512), pb7[0:1, 256:256 + NT], 1.0 / D, None, ALU.mult))
        c.do("vector", lambda e: e.drain(), inc=[("ve", 1)])
        t_rows = c.sems["ve"].n
        c.wait("scalar", [("ve", t_rows)])
        c.do("scalar", lambda e: e.activation(row(1024), row(512), AF.Ln, bias=c.epsc[EPS_MIX][0:1, 0:1]))
        c.do("scalar", lambda e: e.drain())
        c.do("scalar", lambda e: e.activation(row(1024), row(1024), AF.Exp, scale=-0.5),
             inc=[("se", 1)], drain=True)
        t_r = c.sems["se"].n
        c.wait("vector", [("se", t_r)])
        c.do("vector", lambda e: e.tensor_tensor(row(1536), row(0), row(0), ALU.mult))
        c.do("vector", lambda e: e.drain())
        c.do("vector", lambda e: e.tensor_sub(row(1536), row(512), row(1536)))
        c.do("vector", lambda e: e.drain())
        c.do("vector", lambda e: e.tensor_tensor(row(1536), row(1536), row(1024), ALU.mult))
        c.do("vector", lambda e: e.drain())
        c.do("vector", lambda e: e.tensor_tensor(row(1536), row(1536), row(1024), ALU.mult),
             inc=[("ve", 1)], drain=True)
        t_v = c.sems["ve"].n
        c.wait("scalar", [("ve", t_v)])
        c.do("scalar", lambda e: e.activation(row(1536), row(1536), AF.Ln, bias=c.epsc[EPS_LN][0:1, 0:1]))
        c.do("scalar", lambda e: e.drain())
        c.do("scalar", lambda e: e.activation(row(1536), row(1536), AF.Exp, scale=-0.5),
             inc=[("se", 1)], drain=True)
        t_rs = c.sems["se"].n
        c.wait("vector", [("se", t_rs)])
        c.do("vector", lambda e: e.tensor_tensor(row(1024), row(1024), row(1536), ALU.mult))
        c.do("vector", lambda e: e.drain())
        c.do("vector", lambda e: e.tensor_tensor(row(0), row(0), row(1024), ALU.mult))
        c.do("vector", lambda e: e.drain())
        c.do("vector", lambda e: e.tensor_scalar(row(0), row(0), -1.0, None, ALU.mult),
             inc=[("ve", 1)], drain=True)
        t_ab = c.sems["ve"].n
        pb6 = c.psum[6]
        c.wait("tensor", [("ve", t_ab)])
        c.do("tensor", lambda e: e.matmul(pb6[:, 0:NT], c.ones_row, row(1024), start=True, stop=True))
        c.do("tensor", lambda e: e.matmul(pb6[:, 256:256 + NT], c.ones_row, row(0), start=True, stop=True),
             inc=[("pe", 1)])
        t_bc = c.sems["pe"].n
        c.wait("vector", [("pe", t_bc)])
        c.do("vector", lambda e: e.tensor_copy(bA, pb6[:, 0:NT]))
        c.do("vector", lambda e: e.tensor_copy(bB, pb6[:, 256:256 + NT]))
        c.do("vector", lambda e: e.drain())
        c.do("vector", lambda e: e.tensor_mul(x3(tsc), x3(yf), b3(bA)))
        c.do("vector", lambda e: e.drain())
        c.do("vector", lambda e: e.tensor_add(x3(tsc), x3(tsc), b3(bB)))
        c.do("vector", lambda e: e.drain())
        c.do("vector", lambda e: e.tensor_add(x2f, tsc, x1s), inc=[("ve", 1)], drain=True)
        st_t = c.sems["ve"].n
        c.do("sync", (lambda jj: lambda e: e.dma_start(
            out=x2T.rearrange("(k p) t -> p k t", p=128)[:, :, jj * NT:(jj + 1) * NT],
            in_=x3(x2f)))(j), waits=[("ve", st_t)], inc=[("ds", 16)])
    c.wait("sync", [("ds", c.sems["ds"].n)])
    c.wait("gpsimd", [("ds", c.sems["ds"].n)])


PLAN_FNS = {"stage": plan_stage, "A": plan_A, "cc": plan_cc, "B1": plan_B1,
            "B2": plan_B2, "C": plan_C, "D1": plan_D1, "D2": plan_D2, "E": plan_E}


def build_group(phases, dbg_outs=()):
    nc = bass.Bass("TRN2", target_bir_lowering=False, debug=False,
                   detect_race_conditions=False)
    ctx = Ctx(nc)
    produced = set()
    consumed = set()
    for ph in phases:
        produced.update(PHASE_PROD[ph])
        consumed.update(PHASE_CONS[ph])
    ext_in, ext_out = [], []
    for name in consumed:
        if name in EXT_INPUTS:
            sh, dt = EXT_INPUTS[name]
            ctx.dram_t(name, sh, dt, "ExternalInput")
            ext_in.append(name)
        elif name not in produced:
            sh, dt = BOUNCE[name]
            ctx.dram_t(name, sh, dt, "ExternalInput")
            ext_in.append(name)
    for name in produced:
        if name in OUTPUT:
            sh, dt = OUTPUT[name]
            ctx.dram_t(name, sh, dt, "ExternalOutput")
            ext_out.append(name)
        elif (name not in consumed) or (name in dbg_outs):
            sh, dt = BOUNCE[name]
            ctx.dram_t(name, sh, dt, "ExternalOutput")
            ext_out.append(name)
        else:
            sh, dt = BOUNCE[name]
            ctx.dram_t(name, sh, dt, "Internal")
    needs_shift = any(p in phases for p in ("D1", "D2"))
    plan_consts(ctx)
    if needs_shift:
        sd = ctx.dram_t("dqpar", [128, 1], F32, "ExternalInput")
        sd2 = ctx.dram_t("dqpar1m", [128, 1], F32, "ExternalInput")
        ext_in.extend(["dqpar", "dqpar1m"])
        ctx.do("sync", lambda e: e.dma_start(out=ctx.par, in_=sd), inc=[("ds", 16)])
        t = ctx.do("sync", lambda e: e.dma_start(out=ctx.par1m, in_=sd2), inc=[("ds", 16)])[0]
        ctx.wait("vector", [("ds", t)])
    if ("B1" in phases) and ("cc" not in phases):
        plan_load_carryx(ctx)
    for ph in phases:
        PLAN_FNS[ph](ctx)
        _barrier(ctx)
    ctx.emit()
    return nc, ext_in, ext_out


def host_inputs(L0_attn_x, L0_attn_w_qkv, L0_attn_w_swiglu, L0_attn_w_out,
                L0_ffn_gate_up_packed, L0_ffn_gate_up_scales,
                L0_ffn_down_packed, L0_ffn_down_scales):
    per_core = []
    wqkvT = np.ascontiguousarray(np.asarray(L0_attn_w_qkv).T)
    wsT = np.ascontiguousarray(np.asarray(L0_attn_w_swiglu).T)
    woT = np.ascontiguousarray(np.asarray(L0_attn_w_out).T)
    gu_b = np.ascontiguousarray(
        np.repeat(np.asarray(L0_ffn_gate_up_packed).view(np.uint8)
                  .reshape(2 * I, D // 8 * 4).T, 2, axis=0))
    gu_s = np.ascontiguousarray(
        np.repeat(np.asarray(L0_ffn_gate_up_scales).view(np.uint16), 128, axis=0)
        .view(np.float16))
    dn_b = np.ascontiguousarray(
        np.repeat(np.asarray(L0_ffn_down_packed).view(np.uint8)
                  .reshape(D, I // 8 * 4).T, 2, axis=0))
    dn_s = np.ascontiguousarray(
        np.repeat(np.asarray(L0_ffn_down_scales).view(np.uint16), 128, axis=0)
        .view(np.float16))
    par = (np.arange(128) % 2).astype(np.float32).reshape(128, 1)
    dqpar = np.ascontiguousarray(par)
    dqpar1m = np.ascontiguousarray(1.0 - par)
    x = np.asarray(L0_attn_x)
    for cid in range(NCORES):
        b, h = cid // 2, cid % 2
        msel = np.zeros((128, 8), dtype=np.float32)
        if h == 1:
            msel[:, cid - 1] = 1.0
        per_core.append({
            "x0T": np.ascontiguousarray(x[b, h * T:(h + 1) * T, :].T),
            "wqkvT": wqkvT, "wsT": wsT, "woT": woT,
            "gub": gu_b, "gus": gu_s, "dnb": dn_b, "dns": dn_s,
            "msel": msel, "dqpar": dqpar, "dqpar1m": dqpar1m,
        })
    return per_core


def kernel(L0_attn_x, L0_attn_w_qkv, L0_attn_w_swiglu, L0_attn_w_out,
           L0_ffn_gate_up_packed, L0_ffn_gate_up_scales,
           L0_ffn_down_packed, L0_ffn_down_scales):
    per_core = host_inputs(L0_attn_x, L0_attn_w_qkv, L0_attn_w_swiglu, L0_attn_w_out,
                           L0_ffn_gate_up_packed, L0_ffn_gate_up_scales,
                           L0_ffn_down_packed, L0_ffn_down_scales)
    state = [dict(m) for m in per_core]
    final = [None] * NCORES
    dbg = {n: [None] * NCORES for n in DEBUG_OUTS}
    for phases in GROUPS:
        nc, ext_in, ext_out = build_group(phases, DEBUG_OUTS)
        in_maps = [{k: state[cid][k] for k in ext_in} for cid in range(NCORES)]
        res = run_bass_kernel_spmd(nc, in_maps, core_ids=list(range(NCORES)))
        for cid in range(NCORES):
            r = res.results[cid]
            for name in ext_out:
                arr = np.asarray(r[name])
                if name == "x2T":
                    final[cid] = arr
                if name in DEBUG_OUTS:
                    dbg[name][cid] = arr
                state[cid][name] = arr
    if DEBUG_OUTS:
        kernel._dbg = dbg
    kernel._state = state
    if final[0] is None:
        return None
    out = np.empty((B, S, D), dtype=np.float32)
    for cid in range(NCORES):
        b, h = cid // 2, cid % 2
        out[b, h * T:(h + 1) * T, :] = final[cid].T
    return out
